# version 01
# baseline (speedup 1.0000x reference)
import numpy as np
import ml_dtypes
from contextlib import ExitStack

import concourse.bass as bass
import concourse.mybir as mybir
from concourse.bass_utils import run_bass_kernel_spmd

F32 = mybir.dt.float32
BF16 = mybir.dt.bfloat16
AF = mybir.ActivationFunctionType
ALU = mybir.AluOpType
AX = mybir.AxisListType
bf16 = ml_dtypes.bfloat16

MODE = "unfused"

B, S, D = 2, 8192, 1024
H, DH, NPM = 16, 64, 4
NC = 8
T = 2048
NT, TT = 4, 512
GT = 16
SCALE = 0.125
EPS_RMS = float(np.finfo(np.float32).eps)
EPS_LN = 1e-5
P64 = np.concatenate([np.arange(0, 64, 2), np.arange(1, 64, 2)])
SBUF_LIMIT = 205 * 1024


def host_consts():
    inv = 10000.0 ** (-np.arange(0, 64, 2, dtype=np.float64) / 64.0)
    ang = np.arange(S, dtype=np.float64)[:, None] * inv[None, :]
    c32 = np.cos(ang).T
    s32 = np.sin(ang).T
    cosT = np.tile(c32, (4, 1)).astype(np.float32)
    sinT = np.tile(s32, (4, 1)).astype(np.float32)
    i_ = np.arange(128)[:, None]
    j_ = np.arange(512)[None, :]
    masks = np.concatenate(
        [((thr + i_) <= j_).astype(np.float32) for thr in (0, 128, 256, 384)], 1)
    R2 = np.zeros((128, 128), np.float32)
    for base in (0, 64):
        for i in range(32):
            R2[base + i, base + 32 + i] = -1.0
            R2[base + 32 + i, base + i] = 1.0
    return dict(cosT=cosT, sinT=sinT, masks=masks.astype(bf16),
                R2T=np.ascontiguousarray(R2.T).astype(bf16),
                eye=np.eye(128, dtype=np.float32))


def l0_wq_slice(w_qkv, r):
    rows = []
    for part in range(3):
        blk = w_qkv[part * 1024 + 4 * r * 64: part * 1024 + (4 * r + 4) * 64]
        if part < 2:
            blk = blk.reshape(4, 64, 1024)[:, P64, :].reshape(256, 1024)
        rows.append(blk)
    return np.ascontiguousarray(np.concatenate(rows, 0).T)


def l1_wq_slice(w_qkv, r):
    cols = [w_qkv[:, part * 1024 + 4 * r * 64: part * 1024 + (4 * r + 4) * 64]
            for part in range(3)]
    return np.ascontiguousarray(np.concatenate(cols, 1))


def l1_bq_slice(b_qkv, r):
    return np.ascontiguousarray(np.concatenate(
        [b_qkv[part * 1024 + 4 * r * 64: part * 1024 + (4 * r + 4) * 64]
         for part in range(3)]))


def pm_slices(pm, r):
    heads = pm[4 * r:4 * r + 4]
    pmk = np.zeros((128, 8), np.float32)
    for i in range(4):
        pc, half = i // 2, i % 2
        pmk[half * 64:(half + 1) * 64, pc * 4:(pc + 1) * 4] = heads[i][:, P64].T
    pmv = np.concatenate([heads[i] for i in range(4)], 1)
    return pmk, np.ascontiguousarray(pmv)



class Sem:
    def __init__(self, h, name):
        self.h, self.name, self.n = h, name, 0


class Buf:
    __slots__ = ("wtok", "rtoks")

    def __init__(self):
        self.wtok = None
        self.rtoks = {}


class Eng:
    def __init__(self, queue, sem, dma=False, skip_self=False, rot=None):
        self.queue = queue
        self.sem = sem
        self.dma = dma
        self.skip_self = skip_self
        self.rot = rot
        self.dma_count = 0


class Prog:
    QUEUES = ("tensor", "vector", "scalar", "gpsimd", "sync")

    def __init__(self, nc, ctx):
        self.nc, self.ctx = nc, ctx
        self.rec = {q: [] for q in self.QUEUES}
        self.seen = {q: {} for q in self.QUEUES}
        self.sems = {}
        for nm in ("PE", "ACT", "DVE", "POO", "CC",
                   "QS0", "QS1", "QS2", "QS3", "QG0", "QG1", "QG2", "QG3"):
            self.sems[nm] = Sem(ctx.enter_context(nc.semaphore(nm.lower())), nm)
        self.TEN = Eng("tensor", self.sems["PE"], skip_self=True)
        self.SCA = Eng("scalar", self.sems["ACT"])
        self.VEC = Eng("vector", self.sems["DVE"])
        self.POO = Eng("gpsimd", self.sems["POO"])
        self.SYN = Eng("sync", None, dma=True,
                       rot=[self.sems[f"QS{i}"] for i in range(4)])
        self.GPS = Eng("gpsimd", None, dma=True,
                       rot=[self.sems[f"QG{i}"] for i in range(4)])
        self.CCE = Eng("gpsimd", self.sems["CC"])
        self.bufs = {}
        self.abase = nc.sbuf_top - SBUF_LIMIT
        assert self.abase >= nc.sbuf_base + 2048, (self.abase, nc.sbuf_base)
        self.p_off = 0
        self.s_off = SBUF_LIMIT
        self.s_reset = SBUF_LIMIT
        self.parena = ctx.enter_context(nc.psum_tensor("parena", [128, 4096], F32))

    def sal(self, name, shape, dtype, scratch=False):
        bpp = int(np.prod(shape[1:])) * mybir.dt.size(dtype)
        bpp = (bpp + 31) & ~31
        if scratch:
            self.s_off -= bpp
            off = self.s_off
            assert self.s_off >= self.p_off, f"SBUF overflow at {name}"
        else:
            off = self.p_off
            self.p_off += bpp
            assert self.p_off <= self.s_reset, f"SBUF overflow at {name}"
        return self.nc.alloc_sbuf_tensor_at(name, shape, dtype,
                                            offset=self.abase + off)

    def bank(self, i, rows=128, cols=512, row0=0):
        assert 0 <= i < 8 and cols <= 512
        return self.parena[row0:row0 + rows, i * 512: i * 512 + cols]

    def phase(self):
        self.s_off = self.s_reset
        for q in self.QUEUES:
            waits = []
            for s in self.sems.values():
                if s.n > 0:
                    w = self._want_q(q, s, s.n)
                    if w:
                        waits.append(w)
            if waits:
                self.rec[q].append((waits, None, None, 0))

    def buf(self, n):
        b = self.bufs.get(n)
        if b is None:
            b = self.bufs[n] = Buf()
        return b

    def _want_q(self, queue, sem, val):
        if val <= 0:
            return None
        seen = self.seen[queue]
        if seen.get(sem.name, 0) >= val:
            return None
        seen[sem.name] = val
        return (sem.h, val)

    def _want(self, eng, sem, val):
        if eng.skip_self and sem is eng.sem:
            return None
        return self._want_q(eng.queue, sem, val)

    def op(self, eng, fn, reads=(), writes=(), inc=True):
        waits = []
        for bn in reads:
            b = self.buf(bn)
            if b.wtok:
                w = self._want(eng, *b.wtok)
                if w:
                    waits.append(w)
        for bn in writes:
            b = self.buf(bn)
            if b.wtok:
                w = self._want(eng, *b.wtok)
                if w:
                    waits.append(w)
            for s, v in b.rtoks.values():
                w = self._want(eng, s, v)
                if w:
                    waits.append(w)
        if eng.dma:
            R = len(eng.rot)
            k = eng.dma_count
            eng.dma_count += 1
            sem = eng.rot[k % R]
            if k >= R:
                w = self._want_q(eng.queue, sem, 16 * (k // R))
                if w:
                    waits.append(w)
            step = 16
        else:
            sem = eng.sem
            step = 1
        self.rec[eng.queue].append((waits, fn, sem.h if inc else None, step))
        if inc:
            sem.n += step
        val = sem.n
        for bn in writes:
            b = self.buf(bn)
            b.wtok = (sem, val)
            b.rtoks.clear()
        for bn in reads:
            self.buf(bn).rtoks[sem.name] = (sem, val)

    def finish(self):
        self.phase()

    def replay(self):
        nc = self.nc
        with nc.Block() as block:
            def make(qname):
                def body(raw):
                    for waits, fn, semh, k in self.rec[qname]:
                        for h, v in waits:
                            raw.wait_ge(h, v)
                        if fn is not None:
                            inst = fn(raw)
                            if semh is not None:
                                inst.then_inc(semh, k)
                return body
            block.tensor(make("tensor"))
            block.vector(make("vector"))
            block.scalar(make("scalar"))
            block.gpsimd(make("gpsimd"))
            block.sync(make("sync"))



def emit_base_consts(P, t, eye_ap):
    t["eye_f32"] = P.sal("eye_f32", [128, 128], F32)
    t["ones_bf"] = P.sal("ones_bf", [128, 2], BF16)
    t["ones_f32"] = P.sal("ones_f32", [1, 128], F32)
    t["onesrow_bf"] = P.sal("onesrow_bf", [1, TT], BF16)
    t["eps"] = P.sal("eps", [128, 4], F32)
    P.op(P.SYN, lambda r: r.dma_start(out=t["eye_f32"][:, :], in_=eye_ap),
         writes=("eye_f32",))
    P.op(P.POO, lambda r: r.memset(t["ones_bf"][:, :], 1.0), writes=("ones_bf",))
    P.op(P.POO, lambda r: r.memset(t["ones_f32"][:, :], 1.0), writes=("ones_f32",))
    P.op(P.POO, lambda r: r.memset(t["onesrow_bf"][:, :], 1.0), writes=("onesrow_bf",))
    for i, v in enumerate((EPS_RMS, EPS_LN, 1e-38, 0.1)):
        P.op(P.POO, lambda r, i=i, v=v: r.memset(t["eps"][:, i:i + 1], v),
             writes=("eps",))


def load_vec_chunked(P, t, dram_ap, n, name, bank=7):
    k = n // 128
    tmp = P.sal(name + "_tmp", [k, 128], F32, scratch=True)
    out = P.sal(name, [128, k], F32)
    pst = P.bank(bank, cols=max(k, 1))
    P.op(P.SYN, lambda r: r.dma_start(
        out=tmp[:, :], in_=dram_ap.rearrange("(c p) -> c p", p=128)),
        writes=(name + "_tmp",))
    P.op(P.TEN, lambda r: r.transpose(pst, tmp[:, :], t["eye_f32"][0:k, 0:k]),
         reads=(name + "_tmp", "eye_f32"), writes=(name + "_ps",))
    P.op(P.VEC, lambda r: r.tensor_copy(out[:, :], pst),
         reads=(name + "_ps",), writes=(name,))
    return out


def emit_load_stream(P, t, x_ap):
    t["S"] = P.sal("S", [128, NC * T], F32)
    for c in range(NC):
        P.op(P.SYN, lambda r, c=c: r.dma_start(
            out=t["S"][:, c * T:(c + 1) * T], in_=x_ap[c * 128:(c + 1) * 128, :]),
            writes=(f"S{c}",))


def emit_store_stream(P, t, out_ap, tag="xo"):
    for c in range(NC):
        P.op(P.SYN, lambda r, c=c: r.dma_start(
            out=out_ap[c * 128:(c + 1) * 128, :], in_=t["S"][:, c * T:(c + 1) * T]),
            reads=(f"S{c}",), writes=(f"{tag}{c}",))


def emit_cast_stream(P, t):
    for c in range(NC):
        P.op(P.VEC, lambda r, c=c: r.tensor_copy(
            t["Xb"][:, c * T:(c + 1) * T], t["S"][:, c * T:(c + 1) * T]),
            reads=(f"S{c}",), writes=(f"Xb{c}",))


def emit_rms_to_dram(P, t, x_ap, w_sb, out_ap):
    xcol = P.sal("rms_x", [128, NC * TT], F32, scratch=True)
    sq = P.sal("rms_sq", [128, TT], BF16, scratch=True)
    row = P.sal("rms_row", [1, TT], F32, scratch=True)
    tmp = P.sal("rms_tmp", [128, TT], F32, scratch=True)
    ob = P.sal("rms_ob", [128, NC * TT], BF16, scratch=True)
    ps = P.bank(0, rows=1)
    pb = P.bank(1)
    for ti in range(NT):
        for c in range(NC):
            P.op(P.SYN, lambda r, c=c, ti=ti: r.dma_start(
                out=xcol[:, c * TT:(c + 1) * TT],
                in_=x_ap[c * 128:(c + 1) * 128, ti * TT:(ti + 1) * TT]),
                writes=(f"rms_x{c}",))
        for c in range(NC):
            P.op(P.SCA, lambda r, c=c: r.activation(
                out=sq[:, :], in_=xcol[:, c * TT:(c + 1) * TT], func=AF.Square),
                reads=(f"rms_x{c}",), writes=("rms_sq",))
            P.op(P.TEN, lambda r, c=c: r.matmul(
                ps, t["ones_bf"][:, 0:1], sq[:, :],
                start=(c == 0), stop=(c == NC - 1), skip_group_check=True),
                reads=("rms_sq", "ones_bf"), writes=("rms_psb",))
        P.op(P.SCA, lambda r: r.activation(
            out=row[0:1, :], in_=ps, func=AF.Ln, bias=t["eps"][0:1, 0:1], scale=1.0 / D),
            reads=("rms_psb",), writes=("rms_rowL",))
        P.op(P.SCA, lambda r: r.activation(
            out=row[0:1, :], in_=row[0:1, :], func=AF.Exp, scale=-0.5),
            reads=("rms_rowL",), writes=("rms_row",))
        P.op(P.TEN, lambda r: r.matmul(
            pb, t["ones_f32"][0:1, :], row[0:1, :], start=True, stop=True),
            reads=("rms_row", "ones_f32"), writes=("rms_pb",))
        for c in range(NC):
            P.op(P.VEC, lambda r, c=c: r.tensor_tensor(
                tmp[:, :], xcol[:, c * TT:(c + 1) * TT], pb, ALU.mult),
                reads=(f"rms_x{c}", "rms_pb"), writes=("rms_tmp",))
            P.op(P.VEC, lambda r, c=c: r.tensor_scalar(
                ob[:, c * TT:(c + 1) * TT], tmp[:, :], w_sb[:, c:c + 1], None,
                ALU.mult), reads=("rms_tmp", "normw"), writes=(f"rms_ob{c}",))
            P.op(P.SYN, lambda r, c=c, ti=ti: r.dma_start(
                out=out_ap[c * 128:(c + 1) * 128, ti * TT:(ti + 1) * TT],
                in_=ob[:, c * TT:(c + 1) * TT]),
                reads=(f"rms_ob{c}",), writes=(f"rms_o{c}{ti}",))


def xg_view(xg_ap, r):
    return xg_ap[r:r + 1, :].rearrange("a (p n) -> (a p) n", p=1024)


def emit_qkv(P, t, xg_ap, wq_ap, bqrow=None, bv_row=None):
    wq = P.sal("wq", [128, NC * 768], BF16, scratch=True)
    xt = P.sal("xt", [128, 2 * NC * TT], BF16, scratch=True)
    for kc in range(NC):
        P.op(P.GPS, lambda r, kc=kc: r.dma_start(
            out=wq[:, kc * 768:(kc + 1) * 768],
            in_=wq_ap[kc * 128:(kc + 1) * 128, :]), writes=(f"wq{kc}",))
    P.op(P.POO, lambda r: r.memset(
        t["VT"][:, :].rearrange("p (s h e) -> p s h e", s=64, h=4)[:, :, :, 64:65],
        1.0), writes=("VT_ones",))
    for g16 in range(GT):
        par = g16 % 2
        xb = [f"xt{par}_{kc}" for kc in range(NC)]
        rnk, lt = g16 // 4, g16 % 4
        for kc in range(NC):
            P.op(P.SYN, lambda r, kc=kc, rnk=rnk, lt=lt, par=par: r.dma_start(
                out=xt[:, (par * NC + kc) * TT:(par * NC + kc + 1) * TT],
                in_=xg_view(xg_ap, rnk)[kc * 128:(kc + 1) * 128,
                                        lt * TT:(lt + 1) * TT]),
                writes=(xb[kc],))
        for dc in range(4):
            pp = dc % 2
            psb = f"qk_ps{pp}"
            ps = P.bank(pp)
            for kc in range(NC):
                P.op(P.TEN, lambda r, dc=dc, kc=kc, ps=ps, par=par: r.matmul(
                    ps, wq[:, kc * 768 + dc * 128: kc * 768 + (dc + 1) * 128],
                    xt[:, (par * NC + kc) * TT:(par * NC + kc + 1) * TT],
                    start=(kc == 0), stop=(kc == NC - 1 and bqrow is None),
                    skip_group_check=True),
                    reads=(f"wq{kc}", xb[kc]), writes=(psb,))
            if bqrow is not None:
                P.op(P.TEN, lambda r, dc=dc, ps=ps: r.matmul(
                    ps, bqrow[0:1, dc * 128:(dc + 1) * 128],
                    t["onesrow_bf"][0:1, 0:TT],
                    start=False, stop=True, skip_group_check=True),
                    reads=("bqrow", "onesrow_bf"), writes=(psb,))
            isq = dc < 2
            dst = t["QT"] if isq else t["KT"]
            dn = ("QT" if isq else "KT") + str(dc % 2)
            col = (dc % 2) * S + g16 * TT
            sc = SCALE if isq else 1.0
            P.op(P.SCA, lambda r, dst=dst, col=col, sc=sc, ps=ps: r.activation(
                out=dst[:, col:col + TT], in_=ps, func=AF.Copy, scale=sc),
                reads=(psb,), writes=(f"{dn}_{g16}",))
        for sub in range(4):
            pp = sub % 2
            pvb = f"v_ps{pp}"
            pv = P.bank(2 + pp, cols=256)
            for kc in range(NC):
                P.op(P.TEN, lambda r, sub=sub, kc=kc, pv=pv, par=par: r.matmul(
                    pv, xt[:, (par * NC + kc) * TT + sub * 128:
                           (par * NC + kc) * TT + (sub + 1) * 128],
                    wq[:, kc * 768 + 512: kc * 768 + 768],
                    start=(kc == 0), stop=(kc == NC - 1 and bv_row is None),
                    skip_group_check=True),
                    reads=(f"wq{kc}", xb[kc]), writes=(pvb,))
            if bv_row is not None:
                P.op(P.TEN, lambda r, pv=pv: r.matmul(
                    pv, t["onesrow_bf"][0:1, 0:128], bv_row[0:1, :],
                    start=False, stop=True, skip_group_check=True),
                    reads=("bv_row", "onesrow_bf"), writes=(pvb,))
            st = g16 * 4 + sub
            P.op(P.VEC, lambda r, st=st, pv=pv: r.tensor_copy(
                t["VT"][:, :].rearrange("p (s h e) -> p s h e", s=64, h=4)
                [:, st, :, 0:64],
                pv.rearrange("p (h e) -> p h e", h=4)),
                reads=(pvb,), writes=(f"VT_{st}",))


def emit_rope(P, t, cos_ap, sin_ap):
    tabs = P.sal("rope_tab", [128, 4 * TT], F32, scratch=True)
    t1 = P.sal("rope_t1", [128, TT], F32, scratch=True)
    t2 = P.sal("rope_t2", [128, TT], F32, scratch=True)
    for g16 in range(GT):
        par = g16 % 2
        cb, sbn = f"rope_c{par}", f"rope_s{par}"
        P.op(P.SYN, lambda r, g16=g16, par=par: r.dma_start(
            out=tabs[:, par * 2 * TT: par * 2 * TT + TT],
            in_=cos_ap[:, g16 * TT:(g16 + 1) * TT]), writes=(cb,))
        P.op(P.SYN, lambda r, g16=g16, par=par: r.dma_start(
            out=tabs[:, par * 2 * TT + TT: (par + 1) * 2 * TT],
            in_=sin_ap[:, g16 * TT:(g16 + 1) * TT]), writes=(sbn,))
        for name in ("QT", "KT"):
            for pc in range(2):
                dst = t[name]
                col = pc * S + g16 * TT
                src_b = f"{name}{pc}_{g16}"
                rot = P.bank(4 + pc)
                P.op(P.TEN, lambda r, dst=dst, col=col, rot=rot: r.matmul(
                    rot, t["R2T"][:, :], dst[:, col:col + TT],
                    start=True, stop=True),
                    reads=(src_b, "R2T"), writes=(f"rope_rot{pc}",))
                P.op(P.VEC, lambda r, dst=dst, col=col, par=par: r.tensor_tensor(
                    t1[:, :], dst[:, col:col + TT],
                    tabs[:, par * 2 * TT: par * 2 * TT + TT], ALU.mult),
                    reads=(src_b, cb), writes=("rope_t1",))
                P.op(P.VEC, lambda r, par=par, rot=rot: r.tensor_tensor(
                    t2[:, :], rot,
                    tabs[:, par * 2 * TT + TT: (par + 1) * 2 * TT], ALU.mult),
                    reads=(f"rope_rot{pc}", sbn), writes=("rope_t2",))
                P.op(P.VEC, lambda r, dst=dst, col=col: r.tensor_tensor(
                    dst[:, col:col + TT], t1[:, :], t2[:, :], ALU.add),
                    reads=("rope_t1", "rope_t2"), writes=(src_b,))


def emit_negc(P, t, with_pm):
    n2 = P.sal("n2", [1, 8], F32, scratch=True)
    sq = P.sal("nrm_sq", [128, S], BF16, scratch=True)
    mx = P.sal("nrm_mx", [1, 2], F32, scratch=True)
    crow = P.sal("crow", [1, 4], F32, scratch=True)
    P.op(P.POO, lambda r: r.memset(n2[:, :], 0.0), writes=("n2",))
    for qk, name in ((0, "QT"), (1, "KT")):
        for pc in range(2):
            src = t[name]
            P.op(P.SCA, lambda r, src=src, pc=pc: r.activation(
                out=sq[:, :], in_=src[:, pc * S:(pc + 1) * S], func=AF.Square),
                reads=tuple(f"{name}{pc}_{g}" for g in range(GT)),
                writes=("nrm_sq",))
            for half in range(2):
                hd = qk * 4 + pc * 2 + half
                for g16 in range(GT):
                    pp = g16 % 2
                    pn = P.bank(6 + pp, rows=1)
                    P.op(P.TEN, lambda r, half=half, g16=g16, pn=pn: r.matmul(
                        pn, t["ones_bf"][half * 64:(half + 1) * 64, 0:1],
                        sq[half * 64:(half + 1) * 64, g16 * TT:(g16 + 1) * TT],
                        start=True, stop=True),
                        reads=("nrm_sq", "ones_bf"), writes=(f"nrm_ps{pp}",))
                    P.op(P.VEC, lambda r, pp=pp, pn=pn: r.reduce_max(
                        mx[0:1, pp:pp + 1], pn, AX.X),
                        reads=(f"nrm_ps{pp}",), writes=(f"nrm_mx{pp}",))
                    P.op(P.VEC, lambda r, hd=hd, pp=pp: r.tensor_tensor(
                        n2[0:1, hd:hd + 1], n2[0:1, hd:hd + 1],
                        mx[0:1, pp:pp + 1], ALU.max),
                        reads=("n2", f"nrm_mx{pp}"), writes=("n2",))
    if with_pm:
        pmsq = P.sal("pmsq", [128, 8], BF16, scratch=True)
        P.op(P.SCA, lambda r: r.activation(
            out=pmsq[:, :], in_=t["pmK"][:, :], func=AF.Square),
            reads=("pmK",), writes=("pmsq",))
        for h4 in range(4):
            pc, half = h4 // 2, h4 % 2
            pn = P.bank(6, rows=1, cols=4)
            P.op(P.TEN, lambda r, pc=pc, half=half, pn=pn: r.matmul(
                pn, t["ones_bf"][half * 64:(half + 1) * 64, 0:1],
                pmsq[half * 64:(half + 1) * 64, pc * 4:(pc + 1) * 4],
                start=True, stop=True),
                reads=("pmsq", "ones_bf"), writes=("nrm_ps0",))
            P.op(P.VEC, lambda r, pn=pn: r.reduce_max(
                mx[0:1, 0:1], pn, AX.X),
                reads=("nrm_ps0",), writes=("nrm_mx0",))
            P.op(P.VEC, lambda r, h4=h4: r.tensor_tensor(
                n2[0:1, 4 + h4:5 + h4], n2[0:1, 4 + h4:5 + h4],
                mx[0:1, 0:1], ALU.max), reads=("n2", "nrm_mx0"), writes=("n2",))
    P.op(P.VEC, lambda r: r.tensor_tensor(
        crow[0:1, :], n2[0:1, 0:4], n2[0:1, 4:8], ALU.mult),
        reads=("n2",), writes=("crow",))
    P.op(P.SCA, lambda r: r.activation(
        out=crow[0:1, :], in_=crow[0:1, :], func=AF.Ln, bias=t["eps"][0:1, 2:3]),
        reads=("crow",), writes=("crowL",))
    P.op(P.SCA, lambda r: r.activation(
        out=crow[0:1, :], in_=crow[0:1, :], func=AF.Exp, scale=0.5),
        reads=("crowL",), writes=("crowE",))
    P.op(P.VEC, lambda r: r.tensor_scalar(
        crow[0:1, :], crow[0:1, :], -1.0, None, ALU.mult),
        reads=("crowE",), writes=("crowN",))
    pb = P.bank(1, cols=4)
    t["negC"] = P.sal("negC", [128, 4], F32)
    P.op(P.TEN, lambda r, pb=pb: r.matmul(
        pb, t["ones_f32"][0:1, :], crow[0:1, :], start=True, stop=True),
        reads=("crowN", "ones_f32"), writes=("negc_ps",))
    P.op(P.VEC, lambda r, pb=pb: r.tensor_copy(t["negC"][:, :], pb),
         reads=("negc_ps",), writes=("negC",))


def emit_pm_prep(P, t, pmk_ap, pmv_ap):
    t["pmK"] = P.sal("pmK", [128, 8], BF16)
    t["pmV"] = P.sal("pmV", [4, 260], BF16)
    pvf = P.sal("pmv_f", [4, 256], BF16, scratch=True)
    P.op(P.GPS, lambda r: r.dma_start(out=t["pmK"][:, :], in_=pmk_ap),
         writes=("pmK",))
    P.op(P.GPS, lambda r: r.dma_start(out=pvf[:, :], in_=pmv_ap),
         writes=("pmv_f",))
    P.op(P.POO, lambda r: r.memset(t["pmV"][:, :], 1.0), writes=("pmV",))
    P.op(P.VEC, lambda r: r.tensor_copy(
        t["pmV"][:, :].rearrange("p (h e) -> p h e", h=4)[:, :, 0:64],
        pvf[:, :].rearrange("p (h e) -> p h e", h=4)),
        reads=("pmv_f",), writes=("pmV",))


def emit_attention(P, t, with_pm, oloc_ap):
    Pt = P.sal("at_p", [128, 2 * TT], BF16, scratch=True)
    Pm = P.sal("at_pm", [128, TT], BF16, scratch=True)
    Ppm = P.sal("at_ppm", [4, TT], BF16, scratch=True)
    rs = P.sal("at_rs", [1, TT], F32, scratch=True)
    avs = P.sal("at_avs", [64, TT], F32, scratch=True)
    ps_av = P.bank(2, rows=65)
    ps_rb = P.bank(3, rows=64)
    for g16 in range(GT):
        nvis = 4 * (g16 + 1)
        for h4 in range(4):
            pc, half = h4 // 2, h4 % 2
            qb = f"QT{pc}_{g16}"
            avb = "at_avb"
            first = True
            if with_pm:
                ps0 = P.bank(0, rows=4)
                P.op(P.TEN, lambda r, pc=pc, half=half, g16=g16, ps0=ps0: r.matmul(
                    ps0, t["pmK"][half * 64:(half + 1) * 64, pc * 4:(pc + 1) * 4],
                    t["QT"][half * 64:(half + 1) * 64,
                            pc * S + g16 * TT: pc * S + (g16 + 1) * TT],
                    start=True, stop=True),
                    reads=("pmK", qb), writes=("at_s0",))
                P.op(P.SCA, lambda r, h4=h4, ps0=ps0: r.activation(
                    out=Ppm[:, :], in_=ps0, func=AF.Exp,
                    bias=t["negC"][0:4, h4:h4 + 1]),
                    reads=("at_s0", "negC"), writes=("at_ppm",))
                P.op(P.TEN, lambda r, h4=h4: r.matmul(
                    ps_av, t["pmV"][:, h4 * 65:(h4 + 1) * 65],
                    Ppm[:, :], start=True, stop=False, skip_group_check=True),
                    reads=("pmV", "at_ppm"), writes=(avb,))
                first = False
            for cc in range(nvis):
                pp = cc % 2
                sbn = f"at_s{pp}"
                pss = P.bank(pp)
                thr = cc - (nvis - 4)
                P.op(P.TEN, lambda r, pc=pc, half=half, cc=cc, g16=g16, pss=pss:
                     r.matmul(
                         pss, t["KT"][half * 64:(half + 1) * 64,
                                      pc * S + cc * 128: pc * S + (cc + 1) * 128],
                         t["QT"][half * 64:(half + 1) * 64,
                                 pc * S + g16 * TT: pc * S + (g16 + 1) * TT],
                         start=True, stop=True),
                     reads=(f"KT{pc}_{cc // 4}", qb), writes=(sbn,))
                P.op(P.SCA, lambda r, h4=h4, pp=pp, pss=pss: r.activation(
                    out=Pt[:, pp * TT:(pp + 1) * TT], in_=pss, func=AF.Exp,
                    bias=t["negC"][:, h4:h4 + 1]),
                    reads=(sbn, "negC"), writes=(f"at_pt{pp}",))
                pbuf = f"at_pt{pp}"
                if thr >= 0:
                    P.op(P.VEC, lambda r, thr=thr, pp=pp: r.tensor_tensor(
                        Pm[:, :], Pt[:, pp * TT:(pp + 1) * TT],
                        t["masks"][:, thr * TT:(thr + 1) * TT], ALU.mult),
                        reads=(f"at_pt{pp}", "masks"), writes=("at_pmm",))
                    pbuf = "at_pmm"
                P.op(P.TEN, lambda r, cc=cc, h4=h4, pp=pp, pbuf=pbuf,
                     first=first, last=(cc == nvis - 1): r.matmul(
                    ps_av,
                    t["VT"][:, cc * 260 + h4 * 65: cc * 260 + (h4 + 1) * 65],
                    (Pm[:, :] if pbuf == "at_pmm" else
                     Pt[:, pp * TT:(pp + 1) * TT]),
                    start=first, stop=last, skip_group_check=True),
                    reads=(f"VT_{cc}", "VT_ones", pbuf), writes=(avb,))
                first = False
            P.op(P.VEC, lambda r: r.reciprocal(rs[0:1, :], ps_av[64:65, :]),
                 reads=(avb,), writes=("at_rs",))
            P.op(P.TEN, lambda r: r.matmul(
                ps_rb, t["ones_f32"][0:1, 0:64], rs[0:1, :],
                start=True, stop=True),
                reads=("at_rs", "ones_f32"), writes=("at_rbb",))
            P.op(P.VEC, lambda r: r.tensor_copy(avs[:, :], ps_av[0:64, :]),
                 reads=(avb,), writes=("at_avs",))
            P.op(P.VEC, lambda r, pc=pc, half=half, g16=g16: r.tensor_tensor(
                t["OT"][half * 64:(half + 1) * 64,
                        pc * S + g16 * TT: pc * S + (g16 + 1) * TT],
                avs[:, :], ps_rb, ALU.mult),
                reads=("at_avs", "at_rbb"), writes=(f"OT{pc}{half}_{g16}",))
    for pc in range(2):
        P.op(P.SYN, lambda r, pc=pc: r.dma_start(
            out=oloc_ap[pc * 128:(pc + 1) * 128, :],
            in_=t["OT"][:, pc * S:(pc + 1) * S]),
            reads=tuple(f"OT{pc}{h}_{g}" for h in range(2) for g in range(GT)),
            writes=(f"oloc{pc}",))


def emit_outproj(P, t, og_ap, wo_ap, bo_sb=None):
    otg = P.sal("otg", [128, NC * T], BF16, scratch=True)
    wo = P.sal("wo", [128, NC * 1024], BF16, scratch=True)
    for c in range(NC):
        P.op(P.SYN, lambda r, c=c: r.dma_start(
            out=otg[:, c * T:(c + 1) * T], in_=og_ap[c * 128:(c + 1) * 128, :]),
            writes=(f"otg{c}",))
        P.op(P.GPS, lambda r, c=c: r.dma_start(
            out=wo[:, c * 1024:(c + 1) * 1024],
            in_=wo_ap[c * 128:(c + 1) * 128, :]), writes=(f"wo{c}",))
    for dc in range(NC):
        for ti in range(NT):
            pr = (dc * NT + ti) % 2
            psb = f"op_ps{pr}"
            pp = P.bank(pr)
            for kc in range(NC):
                P.op(P.TEN, lambda r, dc=dc, ti=ti, kc=kc, pp=pp: r.matmul(
                    pp, wo[:, kc * 1024 + dc * 128: kc * 1024 + (dc + 1) * 128],
                    otg[:, kc * T + ti * TT: kc * T + (ti + 1) * TT],
                    start=(kc == 0), stop=(kc == NC - 1), skip_group_check=True),
                    reads=(f"wo{kc}", f"otg{kc}"), writes=(psb,))
            ssl = slice(dc * T + ti * TT, dc * T + (ti + 1) * TT)
            if bo_sb is None:
                P.op(P.VEC, lambda r, ssl=ssl, pp=pp: r.tensor_tensor(
                    t["S"][:, ssl], t["S"][:, ssl], pp, ALU.add),
                    reads=(f"S{dc}", psb), writes=(f"S{dc}",))
            else:
                P.op(P.VEC, lambda r, ssl=ssl, pp=pp, dc=dc: r.scalar_tensor_tensor(
                    t["S"][:, ssl], pp, bo_sb[:, dc:dc + 1], t["S"][:, ssl],
                    ALU.add, ALU.add),
                    reads=(f"S{dc}", psb, "bo"), writes=(f"S{dc}",))


def emit_layernorm(P, t, xb_dram=None):
    xb = P.sal("ln_xb", [128, TT], BF16, scratch=True)
    sq = P.sal("ln_sq", [128, TT], BF16, scratch=True)
    mrow = P.sal("ln_m", [1, TT], F32, scratch=True)
    vrow = P.sal("ln_v", [1, TT], F32, scratch=True)
    rrow = P.sal("ln_r", [1, TT], F32, scratch=True)
    mr = P.sal("ln_mr", [1, TT], F32, scratch=True)
    tmp = P.sal("ln_tmp", [128, TT], F32, scratch=True)
    ps_s = P.bank(4, rows=1)
    ps_q = P.bank(5, rows=1)
    pbr = P.bank(6)
    pbm = P.bank(7)
    for ti in range(NT):
        for c in range(NC):
            ssl = slice(c * T + ti * TT, c * T + (ti + 1) * TT)
            P.op(P.VEC, lambda r, ssl=ssl: r.tensor_copy(xb[:, :], t["S"][:, ssl]),
                 reads=(f"S{c}",), writes=("ln_xb",))
            P.op(P.SCA, lambda r, ssl=ssl: r.activation(
                out=sq[:, :], in_=t["S"][:, ssl], func=AF.Square),
                reads=(f"S{c}",), writes=("ln_sq",))
            P.op(P.TEN, lambda r, c=c: r.matmul(
                ps_s, t["ones_bf"][:, 0:1], xb[:, :],
                start=(c == 0), stop=(c == NC - 1), skip_group_check=True),
                reads=("ln_xb", "ones_bf"), writes=("ln_pss",))
            P.op(P.TEN, lambda r, c=c: r.matmul(
                ps_q, t["ones_bf"][:, 1:2], sq[:, :],
                start=(c == 0), stop=(c == NC - 1), skip_group_check=True),
                reads=("ln_sq", "ones_bf"), writes=("ln_psq",))
        P.op(P.SCA, lambda r: r.activation(
            out=mrow[0:1, :], in_=ps_s, func=AF.Copy, scale=1.0 / D),
            reads=("ln_pss",), writes=("ln_m",))
        P.op(P.SCA, lambda r: r.activation(
            out=vrow[0:1, :], in_=ps_q, func=AF.Copy, scale=1.0 / D),
            reads=("ln_psq",), writes=("ln_v",))
        P.op(P.VEC, lambda r: r.tensor_tensor(
            mr[0:1, :], mrow[0:1, :], mrow[0:1, :], ALU.mult),
            reads=("ln_m",), writes=("ln_m2",))
        P.op(P.VEC, lambda r: r.tensor_tensor(
            vrow[0:1, :], vrow[0:1, :], mr[0:1, :], ALU.subtract),
            reads=("ln_v", "ln_m2"), writes=("ln_v2",))
        P.op(P.SCA, lambda r: r.activation(
            out=rrow[0:1, :], in_=vrow[0:1, :], func=AF.Ln, bias=t["eps"][0:1, 1:2]),
            reads=("ln_v2",), writes=("ln_rL",))
        P.op(P.SCA, lambda r: r.activation(
            out=rrow[0:1, :], in_=rrow[0:1, :], func=AF.Exp, scale=-0.5),
            reads=("ln_rL",), writes=("ln_r",))
        P.op(P.VEC, lambda r: r.tensor_tensor(
            mr[0:1, :], mrow[0:1, :], rrow[0:1, :], ALU.mult),
            reads=("ln_m", "ln_r"), writes=("ln_mr",))
        P.op(P.TEN, lambda r: r.matmul(
            pbr, t["ones_f32"][0:1, :], rrow[0:1, :], start=True, stop=True),
            reads=("ln_r", "ones_f32"), writes=("ln_pbr",))
        P.op(P.TEN, lambda r: r.matmul(
            pbm, t["ones_f32"][0:1, :], mr[0:1, :], start=True, stop=True),
            reads=("ln_mr", "ones_f32"), writes=("ln_pbm",))
        for c in range(NC):
            ssl = slice(c * T + ti * TT, c * T + (ti + 1) * TT)
            P.op(P.VEC, lambda r, ssl=ssl: r.tensor_tensor(
                tmp[:, :], t["S"][:, ssl], pbr, ALU.mult),
                reads=(f"S{c}", "ln_pbr"), writes=("ln_tmp",))
            P.op(P.VEC, lambda r, ssl=ssl: r.tensor_tensor(
                t["S"][:, ssl], tmp[:, :], pbm, ALU.subtract),
                reads=("ln_tmp", "ln_pbm"), writes=(f"S{c}",))
            if "Xb" in t:
                P.op(P.VEC, lambda r, ssl=ssl: r.tensor_copy(
                    t["Xb"][:, ssl], t["S"][:, ssl]),
                    reads=(f"S{c}",), writes=(f"Xb{c}",))
                if xb_dram is not None:
                    P.op(P.SYN, lambda r, ssl=ssl, c=c, ti=ti: r.dma_start(
                        out=xb_dram[c * 128:(c + 1) * 128, ti * TT:(ti + 1) * TT],
                        in_=t["Xb"][:, ssl]),
                        reads=(f"Xb{c}",), writes=(f"xbd{c}{ti}",))
            elif xb_dram is not None:
                P.op(P.VEC, lambda r, ssl=ssl: r.tensor_copy(
                    xb[:, :], t["S"][:, ssl]),
                    reads=(f"S{c}",), writes=("ln_xb",))
                P.op(P.SYN, lambda r, c=c, ti=ti: r.dma_start(
                    out=xb_dram[c * 128:(c + 1) * 128, ti * TT:(ti + 1) * TT],
                    in_=xb[:, :]),
                    reads=("ln_xb",), writes=(f"xbd{c}{ti}",))


def emit_mean_partial(P, t, msum_ap):
    ms = P.sal("msum", [128, NC], F32, scratch=True)
    for c in range(NC):
        P.op(P.VEC, lambda r, c=c: r.reduce_sum(
            ms[:, c:c + 1], t["S"][:, c * T:(c + 1) * T], AX.X),
            reads=(f"S{c}",), writes=("msum",))
    P.op(P.SYN, lambda r: r.dma_start(
        out=msum_ap.rearrange("x (p c) -> (x p) c", p=128), in_=ms[:, :]),
        reads=("msum",), writes=("msum_d",))



def emit_l0ffn_weights(P, t, aps):
    w = {}
    sizes = dict(cn1=(1024, 1024), cn2=(1024, 512), cn3=(512, 256),
                 gc1=(1024, 512), gc2=(512, 256))
    for nm, (din, dout) in sizes.items():
        kk = din // 128
        w[nm] = P.sal("w_" + nm, [128, kk * dout], BF16)
        for kc in range(kk):
            P.op(P.GPS, lambda r, nm=nm, kc=kc, dout=dout: r.dma_start(
                out=w[nm][:, kc * dout:(kc + 1) * dout],
                in_=aps[nm][kc * 128:(kc + 1) * 128, :]), writes=("w_" + nm,))
    for nm in ("cn4", "gc3"):
        w[nm] = P.sal("w_" + nm, [128, 2], BF16)
        P.op(P.GPS, lambda r, nm=nm: r.dma_start(
            out=w[nm][:, :], in_=aps[nm].rearrange("(c p) x -> p (c x)", p=128),
            allow_slow_non_contiguous=True),
            writes=("w_" + nm,))
    vecs = {}
    for nm, n in (("cnb1", 1024), ("cnb2", 512), ("cnb3", 256),
                  ("gcb1", 512), ("gcb2", 256)):
        vecs[nm] = load_vec_chunked(P, t, aps[nm], n, "v_" + nm)
    for nm in ("cnb4", "gcb3", "ar"):
        vecs[nm] = P.sal("v_" + nm, [1, 1], F32)
        P.op(P.SYN, lambda r, nm=nm: r.dma_start(
            out=vecs[nm][0:1, 0:1], in_=aps[nm]), writes=("v_" + nm,))
    return w, vecs


def emit_l0ffn(P, t, msg_ap, w, vecs):
    msg = P.sal("ffn_msg", [128, 32], F32, scratch=True)
    mean = P.sal("ffn_mean", [128, NC], F32, scratch=True)
    meanb = P.sal("ffn_meanb", [128, NC], BF16, scratch=True)
    g1 = P.sal("gf_g1", [128, 4], BF16, scratch=True)
    g2 = P.sal("gf_g2", [128, 2], BF16, scratch=True)
    gf = P.sal("gf", [1, 4], F32, scratch=True)
    P.op(P.SYN, lambda r: r.dma_start(
        out=msg[:, :].rearrange("p (r c) -> p r c", r=4),
        in_=msg_ap.rearrange("r (p c) -> p r c", p=128)),
        writes=("ffn_msg",))
    P.op(P.VEC, lambda r: r.reduce_sum(
        mean[:, :], msg[:, :].rearrange("p (r c) -> p c r", r=4), AX.X),
        reads=("ffn_msg",), writes=("ffn_mean",))
    P.op(P.VEC, lambda r: r.tensor_scalar(
        meanb[:, :], mean[:, :], 1.0 / S, None, ALU.mult),
        reads=("ffn_mean",), writes=("ffn_meanb",))
    pgf = P.bank(4, cols=4)
    for dc in range(4):
        P.op(P.TEN, lambda r, dc=dc: [r.matmul(
            pgf[:, dc:dc + 1],
            w["gc1"][:, kc * 512 + dc * 128: kc * 512 + dc * 128 + 128],
            meanb[:, kc:kc + 1], start=(kc == 0), stop=(kc == NC - 1),
            skip_group_check=True) for kc in range(NC)][-1],
            reads=("w_gc1", "ffn_meanb"), writes=("gf_ps",))
    for dc in range(4):
        P.op(P.SCA, lambda r, dc=dc: r.activation(
            out=g1[:, dc:dc + 1], in_=pgf[:, dc:dc + 1], func=AF.Gelu,
            bias=vecs["gcb1"][:, dc:dc + 1]),
            reads=("gf_ps", "v_gcb1"), writes=("gf_g1",))
    pgf2 = P.bank(4, cols=2)
    for dc in range(2):
        P.op(P.TEN, lambda r, dc=dc: [r.matmul(
            pgf2[:, dc:dc + 1],
            w["gc2"][:, kc * 256 + dc * 128: kc * 256 + dc * 128 + 128],
            g1[:, kc:kc + 1], start=(kc == 0), stop=(kc == 3),
            skip_group_check=True) for kc in range(4)][-1],
            reads=("w_gc2", "gf_g1"), writes=("gf_ps2",))
    for dc in range(2):
        P.op(P.SCA, lambda r, dc=dc: r.activation(
            out=g2[:, dc:dc + 1], in_=pgf2[:, dc:dc + 1], func=AF.Gelu,
            bias=vecs["gcb2"][:, dc:dc + 1]),
            reads=("gf_ps2", "v_gcb2"), writes=("gf_g2",))
    pgf3 = P.bank(4, rows=1, cols=1)
    P.op(P.TEN, lambda r: [r.matmul(
        pgf3, w["gc3"][:, kc:kc + 1], g2[:, kc:kc + 1],
        start=(kc == 0), stop=(kc == 1), skip_group_check=True)
        for kc in range(2)][-1],
        reads=("w_gc3", "gf_g2"), writes=("gf_ps3",))
    P.op(P.SCA, lambda r: r.activation(
        out=gf[0:1, 0:1], in_=pgf3, func=AF.Sigmoid,
        bias=vecs["gcb3"][0:1, 0:1]),
        reads=("gf_ps3", "v_gcb3"), writes=("gf",))
    P.op(P.SCA, lambda r: r.activation(
        out=gf[0:1, 1:2], in_=gf[0:1, 0:1], func=AF.Copy, scale=0.7),
        reads=("gf",), writes=("gf7",))
    P.op(P.SCA, lambda r: r.activation(
        out=gf[0:1, 2:3], in_=vecs["ar"][0:1, 0:1], func=AF.Copy, scale=2.0),
        reads=("v_ar",), writes=("gf2a",))
    P.op(P.VEC, lambda r: r.tensor_scalar(
        gf[0:1, 3:4], vecs["ar"][0:1, 0:1], -1.0, 0.1, ALU.mult, ALU.add),
        reads=("v_ar",), writes=("gf1a",))
    W = P.sal("W", [128, NC * T], F32)
    for c in range(NC):
        P.op(P.VEC, lambda r, c=c: r.tensor_copy(
            W[:, c * T:(c + 1) * T], t["S"][:, c * T:(c + 1) * T]),
            reads=(f"S{c}",), writes=(f"W{c}",))
    wb = P.sal("ffn_wb", [128, NC * TT], BF16, scratch=True)
    h1 = P.sal("ffn_h1", [128, NC * TT], BF16, scratch=True)
    h2 = P.sal("ffn_h2", [128, 4 * TT], BF16, scratch=True)
    h3 = P.sal("ffn_h3", [128, 2 * TT], BF16, scratch=True)
    tf = P.sal("cn_tf", [1, TT], F32, scratch=True)
    tmp = P.sal("cn_tmp", [128, TT], F32, scratch=True)
    tmp2 = P.sal("cn_tmp2", [128, TT], F32, scratch=True)
    pr4 = P.bank(2, rows=1)
    cpb = P.bank(3)
    for cyc in range(3):
        for ti in range(NT):
            for c in range(NC):
                P.op(P.VEC, lambda r, c=c, ti=ti: r.tensor_copy(
                    wb[:, c * TT:(c + 1) * TT],
                    W[:, c * T + ti * TT: c * T + (ti + 1) * TT]),
                    reads=(f"W{c}",), writes=(f"ffn_wb{c}",))
            for dc in range(NC):
                pr = dc % 2
                pcn = P.bank(pr)
                P.op(P.TEN, lambda r, dc=dc, pcn=pcn: [r.matmul(
                    pcn, w["cn1"][:, kc * 1024 + dc * 128: kc * 1024 + (dc + 1) * 128],
                    wb[:, kc * TT:(kc + 1) * TT], start=(kc == 0), stop=(kc == 7),
                    skip_group_check=True) for kc in range(NC)][-1],
                    reads=("w_cn1",) + tuple(f"ffn_wb{kc}" for kc in range(NC)),
                    writes=(f"cn_ps{pr}",))
                P.op(P.SCA, lambda r, dc=dc, pcn=pcn: r.activation(
                    out=h1[:, dc * TT:(dc + 1) * TT], in_=pcn,
                    func=AF.Gelu, bias=vecs["cnb1"][:, dc:dc + 1]),
                    reads=(f"cn_ps{pr}", "v_cnb1"), writes=(f"ffn_h1{dc}",))
            for dc in range(4):
                pr = dc % 2
                pcn = P.bank(pr)
                P.op(P.TEN, lambda r, dc=dc, pcn=pcn: [r.matmul(
                    pcn, w["cn2"][:, kc * 512 + dc * 128: kc * 512 + (dc + 1) * 128],
                    h1[:, kc * TT:(kc + 1) * TT], start=(kc == 0), stop=(kc == 7),
                    skip_group_check=True) for kc in range(NC)][-1],
                    reads=("w_cn2",) + tuple(f"ffn_h1{kc}" for kc in range(NC)),
                    writes=(f"cn_ps{pr}",))
                P.op(P.SCA, lambda r, dc=dc, pcn=pcn: r.activation(
                    out=h2[:, dc * TT:(dc + 1) * TT], in_=pcn,
                    func=AF.Gelu, bias=vecs["cnb2"][:, dc:dc + 1]),
                    reads=(f"cn_ps{pr}", "v_cnb2"), writes=(f"ffn_h2{dc}",))
            for dc in range(2):
                pr = dc % 2
                pcn = P.bank(pr)
                P.op(P.TEN, lambda r, dc=dc, pcn=pcn: [r.matmul(
                    pcn, w["cn3"][:, kc * 256 + dc * 128: kc * 256 + (dc + 1) * 128],
                    h2[:, kc * TT:(kc + 1) * TT], start=(kc == 0), stop=(kc == 3),
                    skip_group_check=True) for kc in range(4)][-1],
                    reads=("w_cn3",) + tuple(f"ffn_h2{kc}" for kc in range(4)),
                    writes=(f"cn_ps{pr}",))
                P.op(P.SCA, lambda r, dc=dc, pcn=pcn: r.activation(
                    out=h3[:, dc * TT:(dc + 1) * TT], in_=pcn,
                    func=AF.Gelu, bias=vecs["cnb3"][:, dc:dc + 1]),
                    reads=(f"cn_ps{pr}", "v_cnb3"), writes=(f"ffn_h3{dc}",))
            P.op(P.TEN, lambda r: [r.matmul(
                pr4, w["cn4"][:, kc:kc + 1], h3[:, kc * TT:(kc + 1) * TT],
                start=(kc == 0), stop=(kc == 1), skip_group_check=True)
                for kc in range(2)][-1],
                reads=("w_cn4", "ffn_h30", "ffn_h31"), writes=("cn_pr4",))
            P.op(P.SCA, lambda r: r.activation(
                out=tf[0:1, :], in_=pr4, func=AF.Sigmoid,
                bias=vecs["cnb4"][0:1, 0:1]),
                reads=("cn_pr4", "v_cnb4"), writes=("cn_tf",))
            P.op(P.VEC, lambda r: r.tensor_scalar(
                tf[0:1, :], tf[0:1, :], 0.3, gf[0:1, 1:2], ALU.mult, ALU.add),
                reads=("cn_tf", "gf7"), writes=("cn_tf2",))
            P.op(P.VEC, lambda r: r.tensor_scalar(
                tf[0:1, :], tf[0:1, :], gf[0:1, 2:3], gf[0:1, 3:4],
                ALU.mult, ALU.add),
                reads=("cn_tf2", "gf2a", "gf1a"), writes=("cn_cp",))
            P.op(P.TEN, lambda r: r.matmul(
                cpb, t["ones_f32"][0:1, :], tf[0:1, :], start=True, stop=True),
                reads=("cn_cp", "ones_f32"), writes=("cn_cpb",))
            for a in range(4):
                bq = a + 4
                s_a = 1.0 if a < 2 else -1.0
                s_b = -1.0 if a < 2 else 1.0
                asl = slice(a * T + ti * TT, a * T + (ti + 1) * TT)
                bsl = slice(bq * T + ti * TT, bq * T + (ti + 1) * TT)
                P.op(P.VEC, lambda r, asl=asl: r.tensor_copy(tmp[:, :], W[:, asl]),
                     reads=(f"W{a}",), writes=("cn_tmp",))
                P.op(P.VEC, lambda r, bsl=bsl, s_a=s_a: r.scalar_tensor_tensor(
                    tmp2[:, :], W[:, bsl], s_a, cpb, ALU.mult, ALU.mult),
                    reads=(f"W{bq}", "cn_cpb"), writes=("cn_tmp2",))
                P.op(P.VEC, lambda r, asl=asl: r.tensor_tensor(
                    W[:, asl], W[:, asl], tmp2[:, :], ALU.add),
                    reads=(f"W{a}", "cn_tmp2"), writes=(f"W{a}",))
                P.op(P.VEC, lambda r, s_b=s_b: r.scalar_tensor_tensor(
                    tmp2[:, :], tmp[:, :], s_b, cpb, ALU.mult, ALU.mult),
                    reads=("cn_tmp", "cn_cpb"), writes=("cn_tmp2",))
                P.op(P.VEC, lambda r, bsl=bsl: r.tensor_tensor(
                    W[:, bsl], W[:, bsl], tmp2[:, :], ALU.add),
                    reads=(f"W{bq}", "cn_tmp2"), writes=(f"W{bq}",))
    for c in range(NC):
        P.op(P.VEC, lambda r, c=c: r.tensor_tensor(
            t["S"][:, c * T:(c + 1) * T], t["S"][:, c * T:(c + 1) * T],
            W[:, c * T:(c + 1) * T], ALU.add),
            reads=(f"S{c}", f"W{c}"), writes=(f"S{c}",))



def emit_moe_router(P, t, rw_ap):
    nc = P.nc
    rw = P.sal("rw", [128, NC * 8], BF16, scratch=True)
    esb = P.sal("rt_e", [128, 8], F32, scratch=True)
    m1 = P.sal("rt_m1", [128, 1], F32, scratch=True)
    comb = P.sal("rt_comb", [128, 8], F32, scratch=True)
    combT = P.sal("rt_combT", [8, 16 * 128], F32, scratch=True)
    prt = P.bank(5, cols=8)
    ptr = P.bank(6, rows=8, cols=128)
    for kc in range(NC):
        P.op(P.GPS, lambda r, kc=kc: r.dma_start(
            out=rw[:, kc * 8:(kc + 1) * 8],
            in_=rw_ap[kc * 128:(kc + 1) * 128, :]), writes=("rw",))
    for tt128 in range(16):
        P.op(P.TEN, lambda r, tt128=tt128: [r.matmul(
            prt, t["Xb"][:, kc * T + tt128 * 128: kc * T + (tt128 + 1) * 128],
            rw[:, kc * 8:(kc + 1) * 8], start=(kc == 0), stop=(kc == NC - 1),
            skip_group_check=True) for kc in range(NC)][-1],
            reads=("rw",) + tuple(f"Xb{kc}" for kc in range(NC)),
            writes=("rt_ps",))
        P.op(P.VEC, lambda r: r.reduce_max(m1[:, :], prt, AX.X),
             reads=("rt_ps",), writes=("rt_m1",))
        P.op(P.VEC, lambda r: r.tensor_scalar(
            m1[:, :], m1[:, :], -1.0, None, ALU.mult),
            reads=("rt_m1",), writes=("rt_m1n",))
        P.op(P.SCA, lambda r: r.activation(
            out=esb[:, :], in_=prt, func=AF.Exp, bias=m1[:, :]),
            reads=("rt_ps", "rt_m1n"), writes=("rt_e",))
        P.op(P.VEC, lambda r: r.reduce_sum(m1[:, :], esb[:, :], AX.X),
             reads=("rt_e",), writes=("rt_s",))
        P.op(P.VEC, lambda r: r.reciprocal(comb[:, 0:1], m1[:, :]),
             reads=("rt_s",), writes=("rt_rc",))
        P.op(P.VEC, lambda r: r.tensor_scalar(
            esb[:, :], esb[:, :], comb[:, 0:1], None, ALU.mult),
            reads=("rt_e", "rt_rc"), writes=("rt_sm",))
        P.op(P.VEC, lambda r: r.reduce_max(m1[:, :], esb[:, :], AX.X),
             reads=("rt_sm",), writes=("rt_mm1",))
        P.op(P.VEC, lambda r: r.tensor_scalar(
            comb[:, :], esb[:, :], m1[:, :], None, ALU.is_ge),
            reads=("rt_sm", "rt_mm1"), writes=("rt_ge",))
        P.op(P.VEC, lambda r: r.scalar_tensor_tensor(
            comb[:, :], comb[:, :], -1e30, esb[:, :], ALU.mult, ALU.add),
            reads=("rt_ge", "rt_sm"), writes=("rt_t",))
        P.op(P.VEC, lambda r: r.reduce_max(m1[:, :], comb[:, :], AX.X),
             reads=("rt_t",), writes=("rt_m2",))
        P.op(P.VEC, lambda r: r.tensor_scalar(
            comb[:, :], esb[:, :], m1[:, :], None, ALU.is_ge),
            reads=("rt_sm", "rt_m2"), writes=("rt_keep",))
        P.op(P.VEC, lambda r: r.tensor_tensor(
            comb[:, :], esb[:, :], comb[:, :], ALU.mult),
            reads=("rt_sm", "rt_keep"), writes=("rt_comb",))
        P.op(P.TEN, lambda r: r.transpose(ptr, comb[:, :], t["eye_f32"][:, :]),
             reads=("rt_comb", "eye_f32"), writes=("rt_tr",))
        P.op(P.VEC, lambda r, tt128=tt128: r.tensor_copy(
            combT[:, tt128 * 128:(tt128 + 1) * 128], ptr),
            reads=("rt_tr",), writes=("rt_combT",))
    cdram = nc.dram_tensor("comb_scratch", [8, T], F32).ap()
    P.op(P.SYN, lambda r: r.dma_start(out=cdram, in_=combT[:, :]),
         reads=("rt_combT",), writes=("rt_cd",))
    return cdram


def emit_moe(P, t, aps):
    cdram = emit_moe_router(P, t, aps["router_w"])
    P.phase()
    crow = P.sal("moe_crow", [1, TT], F32, scratch=True)
    act = P.sal("moe_act", [128, 80 * TT], BF16, scratch=True)
    gsl = P.sal("moe_gw", [128, 2 * 1024], BF16, scratch=True)
    usl = P.sal("moe_uw", [128, 2 * 1024], BF16, scratch=True)
    dsl = P.sal("moe_dw", [128, 2 * 1024], BF16, scratch=True)
    sdsl = P.sal("moe_sdw", [128, 2048], BF16, scratch=True)
    gsil = P.sal("moe_gs", [128, TT], BF16, scratch=True)
    usb = P.sal("moe_us", [128, TT], BF16, scratch=True)
    tm = P.sal("moe_tm", [128, TT], BF16, scratch=True)
    cb_sb = P.sal("moe_cbt", [128, TT], BF16, scratch=True)
    pg = P.bank(0)
    pu = P.bank(1)
    pcb = P.bank(2)
    gv, uv, dv = aps["gate_w"], aps["up_w"], aps["down_w"]
    sg, su, sd = aps["sgw"], aps["suw"], aps["sdw"]

    def gate_up(ti, wsrc_g, wsrc_u, e, fc, par, with_cb):
        gb, ub = f"moe_gw{par}", f"moe_uw{par}"
        P.op(P.GPS, lambda r: r.dma_start(
            out=gsl[:, par * 1024:(par + 1) * 1024].rearrange(
                "p (kc f) -> p kc f", kc=8),
            in_=wsrc_g.rearrange("(kc p) f -> p kc f", p=128)), writes=(gb,))
        P.op(P.GPS, lambda r: r.dma_start(
            out=usl[:, par * 1024:(par + 1) * 1024].rearrange(
                "p (kc f) -> p kc f", kc=8),
            in_=wsrc_u.rearrange("(kc p) f -> p kc f", p=128)), writes=(ub,))
        P.op(P.TEN, lambda r: [r.matmul(
            pg, gsl[:, par * 1024 + kc * 128: par * 1024 + (kc + 1) * 128],
            t["Xb"][:, kc * T + ti * TT: kc * T + (ti + 1) * TT],
            start=(kc == 0), stop=(kc == NC - 1), skip_group_check=True)
            for kc in range(NC)][-1],
            reads=(gb,) + tuple(f"Xb{k}" for k in range(NC)), writes=("moe_pg",))
        P.op(P.TEN, lambda r: [r.matmul(
            pu, usl[:, par * 1024 + kc * 128: par * 1024 + (kc + 1) * 128],
            t["Xb"][:, kc * T + ti * TT: kc * T + (ti + 1) * TT],
            start=(kc == 0), stop=(kc == NC - 1), skip_group_check=True)
            for kc in range(NC)][-1],
            reads=(ub,) + tuple(f"Xb{k}" for k in range(NC)), writes=("moe_pu",))
        P.op(P.SCA, lambda r: r.activation(
            out=gsil[:, :], in_=pg, func=AF.Silu),
            reads=("moe_pg",), writes=("moe_gs",))
        P.op(P.SCA, lambda r: r.activation(
            out=usb[:, :], in_=pu, func=AF.Copy),
            reads=("moe_pu",), writes=("moe_us",))
        ai = e * 8 + fc if e is not None else 64 + fc
        if with_cb:
            P.op(P.VEC, lambda r: r.tensor_tensor(
                tm[:, :], gsil[:, :], usb[:, :], ALU.mult),
                reads=("moe_gs", "moe_us"), writes=("moe_tm",))
            P.op(P.VEC, lambda r, ai=ai: r.tensor_tensor(
                act[:, ai * TT:(ai + 1) * TT], tm[:, :], cb_sb[:, :], ALU.mult),
                reads=("moe_tm", "moe_cbs"), writes=(f"moe_act{ai}",))
        else:
            P.op(P.VEC, lambda r, ai=ai: r.tensor_tensor(
                act[:, ai * TT:(ai + 1) * TT], gsil[:, :], usb[:, :], ALU.mult),
                reads=("moe_gs", "moe_us"), writes=(f"moe_act{ai}",))

    for ti in range(NT):
        for e in range(8):
            P.op(P.SYN, lambda r, e=e, ti=ti: r.dma_start(
                out=crow[0:1, :],
                in_=cdram[e:e + 1, ti * TT:(ti + 1) * TT]),
                reads=("rt_cd",), writes=("moe_crow",))
            P.op(P.TEN, lambda r: r.matmul(
                pcb, t["ones_f32"][0:1, :], crow[0:1, :],
                start=True, stop=True),
                reads=("moe_crow", "ones_f32"), writes=("moe_pcb",))
            P.op(P.VEC, lambda r: r.tensor_copy(cb_sb[:, :], pcb),
                 reads=("moe_pcb",), writes=("moe_cbs",))
            for fc in range(8):
                gate_up(ti, gv[e, :, fc * 128:(fc + 1) * 128],
                        uv[e, :, fc * 128:(fc + 1) * 128], e, fc, fc % 2, True)
        for fc in range(16):
            gate_up(ti, sg[:, fc * 128:(fc + 1) * 128],
                    su[:, fc * 128:(fc + 1) * 128], None, fc, fc % 2, False)
        for hc in range(NC):
            pr = hc % 2
            pdb = f"moe_pd{pr}"
            pd = P.bank(3 + pr)
            for e in range(8):
                par = e % 2
                db = f"moe_dw{par}"
                P.op(P.GPS, lambda r, e=e, hc=hc, par=par: r.dma_start(
                    out=dsl[:, par * 1024:(par + 1) * 1024].rearrange(
                        "p (fc hh) -> p fc hh", fc=8),
                    in_=dv[e, :, hc * 128:(hc + 1) * 128].rearrange(
                        "(fc p) hh -> p fc hh", p=128)), writes=(db,))
                P.op(P.TEN, lambda r, e=e, par=par, pd=pd: [r.matmul(
                    pd, dsl[:, par * 1024 + fc * 128: par * 1024 + (fc + 1) * 128],
                    act[:, (e * 8 + fc) * TT:(e * 8 + fc + 1) * TT],
                    start=(e == 0 and fc == 0), stop=False, skip_group_check=True)
                    for fc in range(8)][-1],
                    reads=(db,) + tuple(f"moe_act{e * 8 + f}" for f in range(8)),
                    writes=(pdb,))
            P.op(P.GPS, lambda r, hc=hc: r.dma_start(
                out=sdsl[:, :].rearrange("p (fc hh) -> p fc hh", fc=16),
                in_=sd[:, hc * 128:(hc + 1) * 128].rearrange(
                    "(fc p) hh -> p fc hh", p=128)), writes=("moe_sdw",))
            P.op(P.TEN, lambda r, pd=pd: [r.matmul(
                pd, sdsl[:, fc * 128:(fc + 1) * 128],
                act[:, (64 + fc) * TT:(64 + fc + 1) * TT],
                start=False, stop=(fc == 15), skip_group_check=True)
                for fc in range(16)][-1],
                reads=("moe_sdw",) + tuple(f"moe_act{64 + f}" for f in range(16)),
                writes=(pdb,))
            ssl = slice(hc * T + ti * TT, hc * T + (ti + 1) * TT)
            P.op(P.VEC, lambda r, ssl=ssl, pd=pd, hc=hc: r.tensor_tensor(
                t["S"][:, ssl], t["S"][:, ssl], pd, ALU.add),
                reads=(f"S{hc}", pdb), writes=(f"S{hc}",))



def build_l1():
    nc = bass.Bass("TRN2", target_bir_lowering=False)
    ctx = ExitStack()
    x = nc.dram_tensor("x", [1024, T], F32, kind="ExternalInput").ap()
    nw = nc.dram_tensor("normw", [1024], F32, kind="ExternalInput").ap()
    eye = nc.dram_tensor("eye", [128, 128], F32, kind="ExternalInput").ap()
    xn = nc.dram_tensor("xnloc", [1024, T], BF16, kind="ExternalOutput").ap()
    with ctx:
        P = Prog(nc, ctx)
        t = {}
        emit_base_consts(P, t, eye)
        w_sb = load_vec_chunked(P, t, nw, 1024, "normw")
        P.phase()
        emit_rms_to_dram(P, t, x, w_sb, xn)
        P.finish()
        P.replay()
    return nc


def build_l2(layer):
    nc = bass.Bass("TRN2", target_bir_lowering=False)
    ctx = ExitStack()
    xg = nc.dram_tensor("xg", [4, 1024 * T], BF16, kind="ExternalInput").ap()
    wq = nc.dram_tensor("wq", [1024, 768], F32, kind="ExternalInput").ap()
    eye = nc.dram_tensor("eye", [128, 128], F32, kind="ExternalInput").ap()
    masks = nc.dram_tensor("masks", [128, 4 * 512], BF16, kind="ExternalInput").ap()
    oloc = nc.dram_tensor("oloc", [256, S], BF16, kind="ExternalOutput").ap()
    if layer == 0:
        cos = nc.dram_tensor("cosT", [128, S], F32, kind="ExternalInput").ap()
        sin = nc.dram_tensor("sinT", [128, S], F32, kind="ExternalInput").ap()
        r2t = nc.dram_tensor("R2T", [128, 128], BF16, kind="ExternalInput").ap()
        pmk = nc.dram_tensor("pm_k", [128, 8], F32, kind="ExternalInput").ap()
        pmv = nc.dram_tensor("pm_v", [4, 256], F32, kind="ExternalInput").ap()
    else:
        bq = nc.dram_tensor("bqk", [512], F32, kind="ExternalInput").ap()
        bv = nc.dram_tensor("bv", [1, 256], F32, kind="ExternalInput").ap()
    with ctx:
        P = Prog(nc, ctx)
        t = {}
        t["QT"] = P.sal("QT", [128, 2 * S], BF16)
        t["KT"] = P.sal("KT", [128, 2 * S], BF16)
        t["VT"] = P.sal("VT", [128, 64 * 260], BF16)
        t["OT"] = P.sal("OT", [128, 2 * S], BF16)
        t["masks"] = P.sal("masks", [128, 4 * 512], BF16)
        emit_base_consts(P, t, eye)
        P.op(P.SYN, lambda r: r.dma_start(out=t["masks"][:, :], in_=masks),
             writes=("masks",))
        bqrow = None
        bv_row = None
        if layer == 1:
            bqrow = P.sal("bqrow", [1, 512], BF16)
            P.op(P.GPS, lambda r: r.dma_start(
                out=bqrow[0:1, :], in_=bq.rearrange("(x n) -> x n", x=1)),
                writes=("bqrow",))
            bv_row = P.sal("bv_row", [1, 256], BF16)
            P.op(P.GPS, lambda r: r.dma_start(out=bv_row[0:1, :], in_=bv),
                 writes=("bv_row",))
        if layer == 0:
            t["R2T"] = P.sal("R2T", [128, 128], BF16)
            P.op(P.GPS, lambda r: r.dma_start(out=t["R2T"][:, :], in_=r2t),
                 writes=("R2T",))
            emit_pm_prep(P, t, pmk, pmv)
        P.phase()
        emit_qkv(P, t, xg, wq, bqrow=bqrow, bv_row=bv_row)
        P.phase()
        if layer == 0:
            emit_rope(P, t, cos, sin)
            P.phase()
        emit_negc(P, t, with_pm=(layer == 0))
        P.phase()
        emit_attention(P, t, with_pm=(layer == 0), oloc_ap=oloc)
        P.finish()
        P.replay()
    return nc


def build_l3(layer):
    nc = bass.Bass("TRN2", target_bir_lowering=False)
    ctx = ExitStack()
    x = nc.dram_tensor("x", [1024, T], F32, kind="ExternalInput").ap()
    og = nc.dram_tensor("og", [1024, T], BF16, kind="ExternalInput").ap()
    wo = nc.dram_tensor("wo", [1024, 1024], F32, kind="ExternalInput").ap()
    eye = nc.dram_tensor("eye", [128, 128], F32, kind="ExternalInput").ap()
    xo = nc.dram_tensor("xo", [1024, T], F32, kind="ExternalOutput").ap()
    if layer == 0:
        msum = nc.dram_tensor("msum", [1, 1024], F32, kind="ExternalOutput").ap()
    else:
        bo = nc.dram_tensor("bo", [1024], F32, kind="ExternalInput").ap()
        xbo = nc.dram_tensor("xbo", [1024, T], BF16, kind="ExternalOutput").ap()
    with ctx:
        P = Prog(nc, ctx)
        t = {}
        emit_base_consts(P, t, eye)
        bo_sb = None
        if layer == 1:
            bo_sb = load_vec_chunked(P, t, bo, 1024, "bo")
        emit_load_stream(P, t, x)
        P.phase()
        emit_outproj(P, t, og, wo, bo_sb=bo_sb)
        P.phase()
        emit_layernorm(P, t, xb_dram=(xbo if layer == 1 else None))
        P.phase()
        if layer == 0:
            emit_mean_partial(P, t, msum)
        emit_store_stream(P, t, xo)
        P.finish()
        P.replay()
    return nc


def build_l4():
    nc = bass.Bass("TRN2", target_bir_lowering=False)
    ctx = ExitStack()
    x = nc.dram_tensor("x", [1024, T], F32, kind="ExternalInput").ap()
    msg = nc.dram_tensor("msg", [4, 1024], F32, kind="ExternalInput").ap()
    eye = nc.dram_tensor("eye", [128, 128], F32, kind="ExternalInput").ap()
    aps = {}
    for nm, shp in (("cn1", [1024, 1024]), ("cn2", [1024, 512]),
                    ("cn3", [512, 256]), ("cn4", [256, 1]),
                    ("gc1", [1024, 512]), ("gc2", [512, 256]), ("gc3", [256, 1])):
        aps[nm] = nc.dram_tensor(nm, shp, F32, kind="ExternalInput").ap()
    for nm, n in (("cnb1", 1024), ("cnb2", 512), ("cnb3", 256),
                  ("gcb1", 512), ("gcb2", 256)):
        aps[nm] = nc.dram_tensor(nm, [n], F32, kind="ExternalInput").ap()
    for nm in ("cnb4", "gcb3", "ar"):
        aps[nm] = nc.dram_tensor(nm, [1, 1], F32, kind="ExternalInput").ap()
    xo = nc.dram_tensor("xo", [1024, T], F32, kind="ExternalOutput").ap()
    xbo = nc.dram_tensor("xbo", [1024, T], BF16, kind="ExternalOutput").ap()
    with ctx:
        P = Prog(nc, ctx)
        t = {}
        emit_base_consts(P, t, eye)
        w, vecs = emit_l0ffn_weights(P, t, aps)
        emit_load_stream(P, t, x)
        P.phase()
        emit_l0ffn(P, t, msg, w, vecs)
        P.phase()
        emit_layernorm(P, t, xb_dram=xbo)
        P.phase()
        emit_store_stream(P, t, xo)
        P.finish()
        P.replay()
    return nc


def build_l7():
    nc = bass.Bass("TRN2", target_bir_lowering=False)
    ctx = ExitStack()
    x = nc.dram_tensor("x", [1024, T], F32, kind="ExternalInput").ap()
    eye = nc.dram_tensor("eye", [128, 128], F32, kind="ExternalInput").ap()
    aps = dict(
        router_w=nc.dram_tensor("router_w", [1024, 8], F32,
                                kind="ExternalInput").ap(),
        gate_w=nc.dram_tensor("gate_w", [8, 1024, 1024], F32,
                              kind="ExternalInput").ap(),
        up_w=nc.dram_tensor("up_w", [8, 1024, 1024], F32,
                            kind="ExternalInput").ap(),
        down_w=nc.dram_tensor("down_w", [8, 1024, 1024], F32,
                              kind="ExternalInput").ap(),
        sgw=nc.dram_tensor("sgw", [1024, 2048], F32, kind="ExternalInput").ap(),
        suw=nc.dram_tensor("suw", [1024, 2048], F32, kind="ExternalInput").ap(),
        sdw=nc.dram_tensor("sdw", [2048, 1024], F32, kind="ExternalInput").ap(),
    )
    y = nc.dram_tensor("y", [1024, T], F32, kind="ExternalOutput").ap()
    with ctx:
        P = Prog(nc, ctx)
        t = {}
        emit_base_consts(P, t, eye)
        t["Xb"] = P.sal("Xb", [128, NC * T], BF16)
        emit_load_stream(P, t, x)
        emit_cast_stream(P, t)
        P.phase()
        emit_moe(P, t, aps)
        P.phase()
        emit_layernorm(P, t)
        P.phase()
        emit_store_stream(P, t, y)
        P.finish()
        P.replay()
    return nc



def _f32(a):
    return np.ascontiguousarray(np.asarray(a, dtype=np.float32))


def kernel(**inp):
    cst = host_consts()
    xfull = _f32(inp["L0_attn_x"])
    xs = [np.ascontiguousarray(xfull[c // 4, (c % 4) * T:(c % 4 + 1) * T, :].T)
          for c in range(8)]

    nw = _f32(inp["L0_attn_norm_weight"])
    nc1 = build_l1()
    res1 = run_bass_kernel_spmd(
        nc1, [dict(x=xs[c], normw=nw, eye=cst["eye"]) for c in range(8)],
        core_ids=list(range(8)))
    xn = [np.asarray(res1.results[c]["xnloc"]) for c in range(8)]

    xg = [np.ascontiguousarray(np.stack(
        [xn[4 * b + r].reshape(-1) for r in range(4)])) for b in range(2)]

    wq0 = [l0_wq_slice(_f32(inp["L0_attn_w_qkv"]), c % 4) for c in range(8)]
    pmk, pmv = {}, {}
    for r in range(4):
        pmk[r], pmv[r] = pm_slices(_f32(inp["L0_attn_persist_mem"]), r)
    nc2 = build_l2(0)
    res2 = run_bass_kernel_spmd(
        nc2, [dict(xg=xg[c // 4], wq=wq0[c], eye=cst["eye"], masks=cst["masks"],
                   cosT=cst["cosT"], sinT=cst["sinT"], R2T=cst["R2T"],
                   pm_k=pmk[c % 4], pm_v=pmv[c % 4]) for c in range(8)],
        core_ids=list(range(8)))
    o0 = [np.asarray(res2.results[c]["oloc"]) for c in range(8)]

    def og_for(os_, c):
        b, r = c // 4, c % 4
        return np.ascontiguousarray(np.concatenate(
            [os_[4 * b + j][:, r * T:(r + 1) * T] for j in range(4)], 0))

    wo0 = np.ascontiguousarray(_f32(inp["L0_attn_w_out"]).T)
    nc3 = build_l3(0)
    res3 = run_bass_kernel_spmd(
        nc3, [dict(x=xs[c], og=og_for(o0, c), wo=wo0, eye=cst["eye"])
              for c in range(8)], core_ids=list(range(8)))
    x1 = [np.asarray(res3.results[c]["xo"]) for c in range(8)]
    ms = [np.asarray(res3.results[c]["msum"]) for c in range(8)]
    msg = [np.ascontiguousarray(np.concatenate(
        [ms[4 * b + r] for r in range(4)], 0)) for b in range(2)]

    l4in = dict(
        cn1=_f32(inp["L0_ffn_cn_w1"]), cn2=_f32(inp["L0_ffn_cn_w2"]),
        cn3=_f32(inp["L0_ffn_cn_w3"]), cn4=_f32(inp["L0_ffn_cn_w4"]),
        gc1=_f32(inp["L0_ffn_gc_w1"]), gc2=_f32(inp["L0_ffn_gc_w2"]),
        gc3=_f32(inp["L0_ffn_gc_w3"]),
        cnb1=_f32(inp["L0_ffn_cn_b1"]), cnb2=_f32(inp["L0_ffn_cn_b2"]),
        cnb3=_f32(inp["L0_ffn_cn_b3"]),
        gcb1=_f32(inp["L0_ffn_gc_b1"]), gcb2=_f32(inp["L0_ffn_gc_b2"]),
        cnb4=_f32(inp["L0_ffn_cn_b4"]).reshape(1, 1),
        gcb3=_f32(inp["L0_ffn_gc_b3"]).reshape(1, 1),
        ar=_f32(inp["L0_ffn_adaptive_range"]).reshape(1, 1),
        eye=cst["eye"])
    nc4 = build_l4()
    res4 = run_bass_kernel_spmd(
        nc4, [dict(x=x1[c], msg=msg[c // 4], **l4in) for c in range(8)],
        core_ids=list(range(8)))
    x2 = [np.asarray(res4.results[c]["xo"]) for c in range(8)]
    x2b = [np.asarray(res4.results[c]["xbo"]) for c in range(8)]

    xg1 = [np.ascontiguousarray(np.stack(
        [x2b[4 * b + r].reshape(-1) for r in range(4)])) for b in range(2)]
    wq1 = [l1_wq_slice(_f32(inp["L1_attn_w_qkv"]), c % 4) for c in range(8)]
    bq1 = [l1_bq_slice(_f32(inp["L1_attn_b_qkv"]), c % 4) for c in range(8)]
    nc5 = build_l2(1)
    res5 = run_bass_kernel_spmd(
        nc5, [dict(xg=xg1[c // 4], wq=wq1[c], eye=cst["eye"], masks=cst["masks"],
                   bqk=np.ascontiguousarray(bq1[c][:512]),
                   bv=np.ascontiguousarray(bq1[c][512:].reshape(1, 256)))
              for c in range(8)], core_ids=list(range(8)))
    o1 = [np.asarray(res5.results[c]["oloc"]) for c in range(8)]

    wo1 = _f32(inp["L1_attn_w_out"])
    bo1 = _f32(inp["L1_attn_b_out"])
    nc6 = build_l3(1)
    res6 = run_bass_kernel_spmd(
        nc6, [dict(x=x2[c], og=og_for(o1, c), wo=wo1, bo=bo1, eye=cst["eye"])
              for c in range(8)], core_ids=list(range(8)))
    x3 = [np.asarray(res6.results[c]["xo"]) for c in range(8)]

    l7in = dict(router_w=_f32(inp["L1_ffn_router_w"]),
                gate_w=_f32(inp["L1_ffn_gate_w"]),
                up_w=_f32(inp["L1_ffn_up_w"]),
                down_w=_f32(inp["L1_ffn_down_w"]),
                sgw=_f32(inp["L1_ffn_shared_gate_w"]),
                suw=_f32(inp["L1_ffn_shared_up_w"]),
                sdw=_f32(inp["L1_ffn_shared_down_w"]),
                eye=cst["eye"])
    nc7 = build_l7()
    res7 = run_bass_kernel_spmd(
        nc7, [dict(x=x3[c], **l7in) for c in range(8)], core_ids=list(range(8)))
    ys = [np.asarray(res7.results[c]["y"], dtype=np.float32) for c in range(8)]

    out = np.empty((B, S, D), np.float32)
    for c in range(8):
        b, r = c // 4, c % 4
        out[b, r * T:(r + 1) * T, :] = ys[c].T
    return out
